# version 01
# baseline (speedup 1.0000x reference)
import math
import numpy as np
import concourse.bass as bass
import concourse.mybir as mybir
from concourse.bass_utils import run_bass_kernel_spmd

F32 = mybir.dt.float32
F16 = mybir.dt.float16
I8 = mybir.dt.int8
AF = mybir.ActivationFunctionType
ALU = mybir.AluOpType
AX = mybir.AxisListType

T, C, H, D, FF = 2048, 2048, 16, 128, 8192
NT = T // 128
RSQ = 1.0 / math.sqrt(float(D))
SQD = math.sqrt(float(D))
EPS = 1e-5
RC = 12582912.0

ISSUER = {"qsync": "sync", "qscalar": "scalar", "qgpsimd": "gpsimd"}
ENGINES = ["sync", "tensor", "vector", "scalar", "gpsimd"]
NQSEM = 8


class Sched:
    def __init__(self):
        self.ops = []
        self.lw = {}
        self.readers = {}

    def _rec(self, eng, q, fn, reads, writes, tag):
        i = len(self.ops)
        deps = set()
        for k in reads:
            if k in self.lw:
                deps.add(self.lw[k])
        for k in writes:
            if k in self.lw:
                deps.add(self.lw[k])
            deps.update(self.readers.get(k, ()))
        deps.discard(i)
        self.ops.append(dict(eng=eng, q=q, fn=fn, deps=deps, inc=False,
                             tag=tag, qidx=0, cum=0))
        for k in reads:
            self.readers.setdefault(k, []).append(i)
        for k in writes:
            self.lw[k] = i
            self.readers[k] = [i]
        return i

    def op(self, eng, fn, reads=(), writes=(), tag=""):
        return self._rec(eng, None, fn, tuple(reads), tuple(writes), tag)

    def dma(self, q, fn, reads=(), writes=(), tag=""):
        return self._rec(ISSUER[q], q, fn, tuple(reads), tuple(writes), tag)

    def barrier(self, extra_writes=(), tag="barrier"):
        keys = list(self.lw.keys()) + list(extra_writes)
        return self._rec("sync", None, lambda e: e.nop(), (), tuple(keys), tag)

    def emit(self, nc, sems):
        ops = self.ops
        qn = {}
        for o in ops:
            if o["q"]:
                o["qidx"] = qn.get(o["q"], 0)
                qn[o["q"]] = o["qidx"] + 1
        for o in ops:
            for d in o["deps"]:
                p = ops[d]
                if p["q"] is None and p["eng"] != o["eng"]:
                    p["inc"] = True
        cum = {}
        for o in ops:
            if o["q"] is None:
                c = cum.get(o["eng"], 0) + (1 if o["inc"] else 0)
                cum[o["eng"]] = c
                o["cum"] = c
        qtotal, etotal = dict(qn), dict(cum)
        per_eng = {e: [] for e in ENGINES}
        for i, o in enumerate(ops):
            per_eng[o["eng"]].append(i)

        def qkey(q, idx):
            return f"Q{q}_{idx % NQSEM}"

        def body_for(ename):
            idxs = per_eng[ename]

            def body(eng):
                seen = {}

                def wait(key, val):
                    if seen.get(key, 0) >= val:
                        return
                    seen[key] = val
                    eng.wait_ge(sems[key], val)

                for i in idxs:
                    o = ops[i]
                    if o["q"] is not None and o["qidx"] >= NQSEM:
                        wait(qkey(o["q"], o["qidx"]),
                             16 * (o["qidx"] // NQSEM))
                    for d in sorted(o["deps"]):
                        p = ops[d]
                        if p["q"] is not None:
                            wait(qkey(p["q"], p["qidx"]),
                                 16 * (p["qidx"] // NQSEM + 1))
                        else:
                            if p["eng"] == ename:
                                continue
                            wait("E" + p["eng"], p["cum"])
                    inst = o["fn"](eng)
                    if o["q"] is not None:
                        inst.then_inc(sems[qkey(o["q"], o["qidx"])], 16)
                    elif o["inc"]:
                        inst.then_inc(sems["E" + ename], 1)
                if ename == "sync":
                    for q, n in qtotal.items():
                        for r in range(min(n, NQSEM)):
                            cnt = (n - 1 - r) // NQSEM + 1
                            eng.wait_ge(sems[f"Q{q}_{r}"], 16 * cnt)
                    for e, n in etotal.items():
                        if e != "sync" and n > 0:
                            eng.wait_ge(sems["E" + e], n)
            return body

        with nc.Block() as blk:
            blk.sync(body_for("sync"))
            blk.tensor(body_for("tensor"))
            blk.vector(body_for("vector"))
            blk.scalar(body_for("scalar"))
            blk.gpsimd(body_for("gpsimd"))


class Arena:
    def __init__(self, nc, name, ncols, dt):
        self.t = nc.alloc_sbuf_tensor(name, [128, ncols], dt)
        self.ne = ncols
        self.cur = 0

    def alloc(self, n, align=16):
        c = (self.cur + align - 1) // align * align
        assert c + n <= self.ne, f"{n} @ {c} > {self.ne}"
        self.cur = c + n
        return c

    def ap(self, col, npart=128, ncols=1, p0=0):
        return bass.AP(self.t, p0 * self.ne + col,
                       [[self.ne, npart], [1, ncols]])


def build(debug=0):
    nc = bass.Bass("TRN2", target_bir_lowering=False, detect_race_conditions=False)
    S = Sched()

    def dram_in(name, shape, dt=F32):
        return nc.dram_tensor(name, shape, dt, kind="ExternalInput").ap()

    x_d = dram_in("x", [T, C])
    xT_d = dram_in("xT", [C, T])
    WT_d = {"q": dram_in("WqT", [C, C]), "k": dram_in("WkT", [C, C]),
            "v": dram_in("WvT", [C, C])}
    WoT_d = dram_in("WoT", [C, C])
    WsT_d = dram_in("WsT", [C, 64])
    bf_d = dram_in("bfrow", [1, 64])
    w1T_d = dram_in("w1T", [C, FF])
    w2T_d = dram_in("w2T", [FF, C])
    b1_d = dram_in("b1row", [1, FF])
    b2_d = dram_in("b2row", [1, C])
    g_d = dram_in("grow", [1, FF])
    be_d = dram_in("brow", [1, FF])
    cosf_d = dram_in("COSF", [128, T], F16)
    sinf_d = dram_in("SINF", [128, T], F16)
    sel_d = dram_in("SEL16", [80, 16 * 128], F16)
    strict_d = dram_in("STRICT", [128, 128], F16)
    nid_d = dram_in("NEGID", [128, 128], F16)
    id32_d = dram_in("ID32", [128, 128], F32)
    onr16_d = dram_in("ONR16", [65, 128], F16)
    onr32_d = dram_in("ONR32", [65, 128], F32)
    onc16_d = dram_in("ONC16", [128, 16], F16)
    onc32_d = dram_in("ONC32", [128, 16], F32)
    zc16_d = dram_in("ZC16", [128, 16], F16)

    out_d = nc.dram_tensor("out", [T, C], F32, kind="ExternalOutput").ap()
    w1q_d = nc.dram_tensor("w1q", [C, FF], F16, kind="Internal").ap()
    w2q_d = nc.dram_tensor("w2q", [FF, C], F16, kind="Internal").ap()
    r1n_d = nc.dram_tensor("r1nscr", [T, C], F16, kind="Internal").ap()
    ot_d = nc.dram_tensor("otscr", [H * 128, T], F16, kind="Internal").ap()

    dbg = {}
    if debug:
        def dbg_out(nm, shape, dt):
            dbg[nm] = nc.dram_tensor(nm, shape, dt, kind="ExternalOutput").ap()
        dbg_out("dbg_negsc", [16, T], F16)
        dbg_out("dbg_negcS", [128, 256], F32)
        dbg_out("dbg_QT", [128, T], F16)
        dbg_out("dbg_KT", [128, T], F16)
        dbg_out("dbg_V", [128, T], F16)
        dbg_out("dbg_OT", [128, T], F16)
        dbg_out("dbg_r1n", [128, C], F16)
        dbg_out("dbg_h0", [128, FF], F16)
        dbg_out("dbg_xq2", [128, FF], F16)
        dbg_out("dbg_hg", [128, FF], F16)
        dbg_out("dbg_sml", [1, 6], F32)
        dbg_out("dbg_swc", [128, 2], F32)
        dbg_out("dbg_wi8", [128, 2048], F32)
        dbg_out("dbg_cvt", [128, 1024], F32)
        dbg_out("dbg_r1", [128, C], F32)
        dbg_out("dbg_st2", [128, 18], F32)
        dbg_out("dbg_tb2", [128, 3], F32)
        dbg_out("dbg_tb3", [128, 2], F32)
        dbg_out("dbg_w1q", [128, FF], F16)

    A16 = Arena(nc, "A16", 84992, F16)
    A32 = Arena(nc, "A32", 7680, F32)
    A8 = Arena(nc, "A8", 8192, I8)

    c_nid = A16.alloc(128)
    c_strict = A16.alloc(128)
    c_onr16 = A16.alloc(128)
    c_onc16 = A16.alloc(16)
    c_zero = A16.alloc(16)
    c_sel = A16.alloc(2048)
    c_cos = A16.alloc(2048)
    c_sin = A16.alloc(2048)
    c_negsc = A16.alloc(2048)
    c_pT = A16.alloc(2048)
    base16 = A16.cur

    c_id32 = A32.alloc(128)
    c_onr32 = A32.alloc(128)
    c_onc32 = A32.alloc(16)
    c_negcS = A32.alloc(256)
    c_sml = A32.alloc(16)
    c_bf = A32.alloc(64)
    base32 = A32.cur

    p_XT = A16.alloc(32768)
    p_OTb = A16.alloc(2048)
    p_Wsl = [A16.alloc(2048) for _ in range(4)]
    p_QT = A16.alloc(2048)
    p_KT = A16.alloc(2048)
    p_VTg = A16.alloc(2048)
    p_V = A16.alloc(2048)
    p_y = [A16.alloc(1024) for _ in range(2)]
    p_ysh = A16.alloc(2064)
    p_diff = A16.alloc(1024)
    p_m = A16.alloc(1024)
    p_gsw = A16.alloc(1024)
    p_Pb = [A16.alloc(1024) for _ in range(2)]
    p_PT = [A16.alloc(512) for _ in range(3)]
    assert A16.cur <= 84992, A16.cur
    p_g = p_diff
    p_wq16 = p_VTg
    p_OTsl = [p_Pb[0], p_Pb[1]]
    p_r1nst = p_KT
    p_sqscr = p_QT

    p_wstr = [A32.alloc(2048) for _ in range(2)]
    p_x32 = p_wstr[0]
    p_r1 = p_wstr[1]
    p_rsum = A32.alloc(2048)
    p_recb = A32.alloc(512)
    p_st = A32.alloc(80)
    p_wp = A32.alloc(128)
    assert A32.cur <= 7680, A32.cur
    p_p48 = p_rsum
    p_sig = p_recb

    A16.cur = base16
    s_slot = [A16.alloc(8192) for _ in range(5)]
    z1 = A16.cur
    s_w1sl = [A16.alloc(4096) for _ in range(2)]
    s_xq1d = A16.alloc(2048)
    s_xq1dT = [A16.alloc(2048) for _ in range(4)]
    s_r1n = [A16.alloc(2048) for _ in range(2)]
    z1end = A16.cur
    A16.cur = z1
    s_w2sl = [A16.alloc(4096) for _ in range(2)]
    s_G = A16.alloc(2048)
    s_B = A16.alloc(2048)
    s_v = A16.alloc(2048)
    s_r2 = [A16.alloc(2048) for _ in range(4)]
    s_r1nc = [A16.alloc(512) for _ in range(2)]
    z1end = max(z1end, A16.cur)
    A16.cur = z1end
    s_rows = A16.alloc(8192)
    s_rows2 = c_sel
    s_srow = c_cos
    assert A16.cur <= 84992, A16.cur

    A32.cur = base32
    s_out = [A32.alloc(512) for _ in range(2)]
    s_tb = A32.alloc(64)
    s_gp = A32.alloc(64)
    s_sq = A32.alloc(16)
    s_sc = A32.alloc(16)
    assert A32.cur <= 7680

    s_xq1i = 0
    s_xq2i = 0

    pBIG = nc.alloc_psum_tensor("pBIG", [128, 2048], F32)
    pST = [nc.alloc_psum_tensor("pST0", [128, 512], F32),
           nc.alloc_psum_tensor("pST1", [128, 512], F32)]
    pOTp = nc.alloc_psum_tensor("pOTp", [128, 512], F32)
    pMIS = nc.alloc_psum_tensor("pMIS", [128, 512], F32)

    def bigap(c0, npart=128, ncols=512, p0=0):
        return bass.AP(pBIG, p0 * 2048 + c0, [[2048, npart], [1, ncols]])

    def psap(t, c0, npart=128, ncols=512, p0=0):
        return bass.AP(t, p0 * 512 + c0, [[512, npart], [1, ncols]])

    def bap(sl, npart=128, ncols=512):
        if sl < 4:
            return bigap(512 * sl, npart, ncols)
        return psap([pST[0], pST[1], pOTp, pMIS][sl - 4], 0, npart, ncols)

    a16, a32 = A16.ap, A32.ap

    def a8ap(col, npart=128, ncols=1):
        return bass.AP(A8.t, col, [[A8.ne, npart], [1, ncols]])

    def hbm(t_ap, off, ap):
        return bass.AP(t_ap.tensor, off, ap)

    for dst, src, key in [
            (a16(c_nid, 128, 128), nid_d, "c_nid"),
            (a16(c_strict, 128, 128), strict_d, "c_strict"),
            (a16(c_onr16, 65, 128), onr16_d, "c_onr16"),
            (a16(c_onc16, 128, 16), onc16_d, "c_onc16"),
            (a16(c_zero, 128, 16), zc16_d, "c_zero"),
            (a16(c_sel, 80, 2048), sel_d, "c_sel"),
            (a16(c_cos, 128, 2048), cosf_d, "c_cos"),
            (a16(c_sin, 128, 2048), sinf_d, "c_sin"),
            (a32(c_id32, 128, 128), id32_d, "c_id32"),
            (a32(c_onr32, 65, 128), onr32_d, "c_onr32"),
            (a32(c_onc32, 128, 16), onc32_d, "c_onc32"),
            (a32(c_bf, 1, 64), bf_d, "c_bf"),
    ]:
        S.dma("qsync", (lambda d=dst, s=src: lambda e: e.dma_start(d, s))(),
              writes=[key], tag="const")

    S.op("vector", lambda e: e.tensor_copy(a16(p_ysh, 128, 1), a16(c_zero, 128, 1)),
         reads=["c_zero"], writes=["yshz"], tag="yshz")

    if debug:
        S.dma("qscalar",
              lambda e: e.dma_start(a32(p_x32, 128, 512),
                                    hbm(x_d, 0, [[C, 128], [1, 512]])),
              writes=["cvt_x"], tag="dbg")
        S.op("scalar",
             lambda e: e.activation(a8ap(0, 128, 512), a32(p_x32, 128, 512),
                                    AF.Copy, scale=13.3),
             reads=["cvt_x"], writes=["cvt_i8"], tag="dbg")
        S.op("vector",
             lambda e: e.tensor_copy(a32(p_x32 + 512, 128, 512),
                                     a8ap(0, 128, 512)),
             reads=["cvt_i8"], writes=["cvt_r"], tag="dbg")
        S.op("vector",
             lambda e: e.tensor_scalar(a8ap(512, 128, 512), a32(p_x32, 128, 512),
                                       13.3, None, ALU.mult),
             reads=["cvt_x", "cvt_i8"], writes=["cvt_i8v"], tag="dbg")
        S.op("vector",
             lambda e: e.tensor_copy(a32(p_x32 + 1024, 128, 512),
                                     a8ap(512, 128, 512)),
             reads=["cvt_i8v"], writes=["cvt_rv"], tag="dbg")
        S.dma("qscalar",
              lambda e: e.dma_start(dbg["dbg_cvt"], a32(p_x32 + 512, 128, 1024)),
              reads=["cvt_r", "cvt_rv"], writes=["cvt_done"], tag="dbg")
        S._rec("sync", None, lambda e: e.nop(),
               ("cvt_done", "cvt_r", "cvt_rv"), ("wstr0", "xqi", "x32"),
               "cvtfence")

    for cb in range(16):
        dst = a16(p_XT + 2048 * cb, 128, 2048)
        src = hbm(xT_d, (128 * cb) * T, [[T, 128], [1, T]])
        S.dma("qgpsimd", (lambda d=dst, s=src: lambda e: e.dma_start(d, s))(),
              writes=[f"XT{cb}"], tag="ldXT")

    wsT_c = p_Pb[0]
    for cb in range(16):
        dst = a16(wsT_c + 64 * cb, 128, 64)
        src = hbm(WsT_d, (128 * cb) * 64, [[64, 128], [1, 64]])
        S.dma("qgpsimd", (lambda d=dst, s=src: lambda e: e.dma_start(d, s))(),
              writes=[f"WsT{cb}"], tag="ldWsT")

    for i in range(NT):
        for cb in range(16):
            S.op("tensor",
                 (lambda i=i, cb=cb, st=(cb == 0): lambda e: e.matmul(
                     psap(pST[0], 0, 128, 64),
                     a16(p_XT + 2048 * cb + 128 * i, 128, 128),
                     a16(wsT_c + 64 * cb, 128, 64), start=st, stop=False))(),
                 reads=[f"XT{cb}", f"WsT{cb}"], writes=["bank4"], tag="zmm")
        S.op("tensor",
             lambda e: e.matmul(psap(pST[0], 0, 128, 64),
                                a32(c_onr32, 1, 128), a32(c_bf, 1, 64),
                                start=False, stop=True),
             reads=["c_bf", "c_onr32"], writes=["bank4"], tag="zbf")
        S.op("scalar",
             (lambda i=i: lambda e: e.activation(
                 a32(p_p48 + 48 * i, 128, 48), psap(pST[0], 0, 128, 48),
                 AF.Sigmoid))(),
             reads=["bank4"], writes=[f"p48_{i}"], tag="sigp")
        S.op("scalar",
             (lambda i=i: lambda e: e.activation(
                 a32(p_sig + 16 * i, 128, 16), psap(pST[0], 48, 128, 16),
                 AF.Sigmoid))(),
             reads=["bank4"], writes=[f"sigf{i}"], tag="sigf")

    S.op("scalar", lambda e: e.activation(a32(p_x32, 128, 256),
                                          a32(p_sig, 128, 256), AF.Ln),
         reads=[f"sigf{i}" for i in range(NT)], writes=["lnf", "wstr0"], tag="lnf")
    S.op("vector", lambda e: e.tensor_scalar(a32(p_x32 + 256, 128, 256),
                                             a32(p_x32, 128, 256), -1.0, None,
                                             ALU.mult),
         reads=["lnf", "wstr0"], writes=["nlf", "wstr0"], tag="neg")

    for i in range(NT):
        for pj in range(3):
            S.op("tensor",
                 (lambda i=i, pj=pj: lambda e: e.transpose(
                     psap(pMIS, 0, 16, 128),
                     a32(p_p48 + 48 * i + 16 * pj, 128, 16),
                     a32(c_id32, 128, 128)))(),
                 reads=[f"p48_{i}", "c_id32"], writes=["bank7"], tag="ptr")
            S.op("vector",
                 (lambda i=i, pj=pj: lambda e: e.tensor_copy(
                     a16(c_pT + 128 * i, 16, 128, p0=32 * pj),
                     psap(pMIS, 0, 16, 128)))(),
                 reads=["bank7"], writes=[f"pT{pj}_{i}"], tag="ptev")
        S.op("tensor",
             (lambda i=i: lambda e: e.transpose(
                 psap(pMIS, 128, 16, 128),
                 a32(p_x32 + 256 + 16 * i, 128, 16), a32(c_id32, 128, 128)))(),
             reads=["nlf", "c_id32", "wstr0"], writes=["bank7"], tag="ntr")
        S.op("vector",
             (lambda i=i: lambda e: e.tensor_copy(
                 a32(p_r1 + 128 * i, 16, 128), psap(pMIS, 128, 16, 128)))(),
             reads=["bank7", "wstr1"], writes=["nlfT", "wstr1"], tag="ntev")

    S.op("vector", lambda e: e.tensor_tensor_scan(
        a32(p_x32, 16, 2048), a32(p_r1, 16, 2048), a32(p_r1, 16, 2048),
        0.0, ALU.add, ALU.bypass),
        reads=["nlfT", "wstr1", "wstr0", "lnf", "nlf"],
        writes=["negc32", "wstr0"], tag="scan")
    S.op("vector", lambda e: e.tensor_scalar(a16(c_negsc, 16, 2048),
                                             a32(p_x32, 16, 2048), -SQD, None,
                                             ALU.mult),
         reads=["negc32", "wstr0"], writes=["negsc"], tag="nsc")
    for b in range(NT):
        S.op("tensor",
             (lambda b=b: lambda e: e.transpose(
                 psap(pMIS, 256, 128, 16), a32(p_x32 + 128 * b, 16, 128),
                 a32(c_id32, 16, 16)))(),
             reads=["negc32", "c_id32", "wstr0"], writes=["bank7"], tag="str")
        S.op("vector",
             (lambda b=b: lambda e: e.tensor_copy(
                 a32(c_negcS + 16 * b, 128, 16), psap(pMIS, 256, 128, 16)))(),
             reads=["bank7"], writes=[f"negcS{b}"], tag="sev")

    if debug:
        S.dma("qscalar", lambda e: e.dma_start(dbg["dbg_negsc"],
                                               a16(c_negsc, 16, 2048)),
              reads=["negsc"], tag="dbg")
        S.dma("qscalar", lambda e: e.dma_start(dbg["dbg_negcS"],
                                               a32(c_negcS, 128, 256)),
              reads=[f"negcS{b}" for b in range(NT)], tag="dbg")

    S.barrier(tag="b_ph1")

    wtiles = []
    for r in range(16):
        for cc in range(4):
            wtiles.append((hbm(w1T_d, (128 * r) * FF + 2048 * cc,
                               [[FF, 128], [1, 2048]]), "w1",
                           hbm(w1q_d, (128 * r) * FF + 2048 * cc,
                               [[FF, 128], [1, 2048]])))
    for r in range(64):
        wtiles.append((hbm(w2T_d, (128 * r) * C, [[C, 128], [1, 2048]]), "w2",
                       hbm(w2q_d, (128 * r) * C, [[C, 128], [1, 2048]])))

    s_wp = p_wp
    wp = {"i": 0}

    def emit_wprep1_chunk(n):
        for _ in range(n):
            ti = wp["i"]
            if ti >= len(wtiles):
                return
            wp["i"] += 1
            src, w, _ = wtiles[ti]
            buf = p_wstr[ti % 2]
            S.dma("qscalar",
                  (lambda b=buf, s=src: lambda e: e.dma_start(a32(b, 128, 2048), s))(),
                  writes=[f"wstr{ti % 2}"], tag="wld1")
            S.op("vector",
                 (lambda b=buf, ti=ti: lambda e: e.tensor_reduce(
                     a32(s_wp + ti, 128, 1), a32(b, 128, 2048), AX.X,
                     ALU.add, apply_absolute_value=True))(),
                 reads=[f"wstr{ti % 2}"], writes=[f"wp{ti}"], tag="wred")

    def emit_head(h):
        for pi, pj in enumerate("qkv"):
            ring = (3 * h + pi) % 4
            src = hbm(WT_d[pj], 128 * h, [[C, 128], [128 * C, 16], [1, 128]])
            dst = bass.AP(A16.t, p_Wsl[ring],
                          [[A16.ne, 128], [128, 16], [1, 128]])
            S.dma("qgpsimd",
                  (lambda d=dst, s=src: lambda e: e.dma_start(d, s))(),
                  writes=[f"Wsl{ring}"], tag="ldW")

        for pi, pj in enumerate("qkv"):
            ring = (3 * h + pi) % 4
            for c2 in range(2):
                bA, bB = (0, 1) if c2 == 0 else (2, 3)
                ckA, ckB = f"bank{bA}", f"bank{bB}"
                ptk = [f"pT{pi}_{i}" for i in range(8 * c2, 8 * c2 + 8)]
                S.op("tensor",
                     (lambda h=h, pi=pi, c2=c2, bA=bA: lambda e: e.matmul(
                         bigap(512 * bA),
                         a16(c_sel + 128 * h, 16, 128, p0=32 * pi),
                         a16(c_pT + 1024 * c2, 16, 512, p0=32 * pi),
                         start=True, stop=True))(),
                     reads=["c_sel"] + ptk, writes=[ckA], tag="pbm")
                S.op("tensor",
                     (lambda h=h, pi=pi, c2=c2, bB=bB: lambda e: e.matmul(
                         bigap(512 * bB),
                         a16(c_sel + 128 * h, 16, 128, p0=32 * pi),
                         a16(c_pT + 1024 * c2 + 512, 16, 512, p0=32 * pi),
                         start=True, stop=True))(),
                     reads=["c_sel"] + ptk, writes=[ckB], tag="pbm")
                S.op("vector",
                     (lambda c2=c2, bA=bA: lambda e: e.tensor_copy(
                         a16(p_Pb[c2], 128, 1024), bigap(512 * bA, 128, 1024)))(),
                     reads=[ckA, ckB], writes=[f"Pb{c2}"], tag="pbe")
                for cb in range(16):
                    S.op("tensor",
                         (lambda ring=ring, cb=cb, c2=c2, bA=bA, st=(cb == 0):
                          lambda e: e.matmul(
                              bigap(512 * bA),
                              a16(p_Wsl[ring] + 128 * cb, 128, 128),
                              a16(p_XT + 2048 * cb + 1024 * c2, 128, 512),
                              start=st, stop=(cb == 15)))(),
                         reads=[f"Wsl{ring}", f"XT{cb}", f"Pb{c2}"],
                         writes=[ckA], tag="prj")
                    S.op("tensor",
                         (lambda ring=ring, cb=cb, c2=c2, bB=bB, st=(cb == 0):
                          lambda e: e.matmul(
                              bigap(512 * bB),
                              a16(p_Wsl[ring] + 128 * cb, 128, 128),
                              a16(p_XT + 2048 * cb + 1024 * c2 + 512, 128, 512),
                              start=st, stop=(cb == 15)))(),
                         reads=[f"Wsl{ring}", f"XT{cb}", f"Pb{c2}"],
                         writes=[ckB], tag="prj")
                yb = p_y[c2]
                S.op("scalar",
                     (lambda yb=yb, bA=bA: lambda e: e.activation(
                         a16(yb, 128, 1024), bigap(512 * bA, 128, 1024),
                         AF.Copy))(),
                     reads=[ckA, ckB], writes=[f"ybuf{c2}"], tag="yev")
                w = 1023 if c2 == 1 else 1024
                S.op("scalar",
                     (lambda c2=c2, bA=bA, w=w: lambda e: e.activation(
                         a16(p_ysh + 1024 * c2 + 1, 128, w),
                         bigap(512 * bA, 128, w), AF.Copy))(),
                     reads=[ckA, ckB], writes=[f"yshb{c2}"], tag="ysev")
                S.op("vector",
                     (lambda c2=c2, yb=yb: lambda e: e.tensor_tensor(
                         a16(p_diff, 128, 1024), a16(p_ysh + 1024 * c2, 128, 1024),
                         a16(yb, 128, 1024), ALU.subtract))(),
                     reads=[f"ybuf{c2}", f"yshb{c2}", "yshz"], writes=["diff"],
                     tag="gd")
                S.op("vector",
                     (lambda c2=c2: lambda e: e.tensor_tensor(
                         a16(p_m, 128, 1024), a16(p_diff, 128, 1024),
                         a16(p_Pb[c2], 128, 1024), ALU.mult))(),
                     reads=["diff", f"Pb{c2}"], writes=["mbuf"], tag="gm")
                gdst = (p_VTg + 1024 * c2) if pj == "v" else p_g
                gkey = f"VTg{c2}" if pj == "v" else "diff"
                S.op("vector",
                     (lambda gdst=gdst, yb=yb: lambda e: e.tensor_tensor(
                         a16(gdst, 128, 1024), a16(yb, 128, 1024),
                         a16(p_m, 128, 1024), ALU.add))(),
                     reads=[f"ybuf{c2}", "mbuf", "diff"], writes=[gkey], tag="gg")
                if pj == "v":
                    for b8 in range(8):
                        b = 8 * c2 + b8
                        S.dma("qsync",
                              (lambda b=b: lambda e: e.dma_start(
                                  a16(p_V + 128 * b, 128, 128),
                                  a16(p_VTg + 128 * b, 128, 128),
                                  transpose=True))(),
                              reads=[gkey], writes=[f"Vb{b}", "xbar"], tag="vtr")
                else:
                    S.dma("qsync",
                          lambda e: e.dma_start(
                              a16(p_gsw, 64, 1024),
                              bass.AP(A16.t, 64 * A16.ne + p_g,
                                      [[A16.ne, 64], [1, 1024]])),
                          reads=["diff"], writes=["gsw", "xbar"], tag="swp")
                    S.dma("qsync",
                          lambda e: e.dma_start(
                              bass.AP(A16.t, 64 * A16.ne + p_gsw,
                                      [[A16.ne, 64], [1, 1024]]),
                              a16(p_g, 64, 1024)),
                          reads=["diff"], writes=["gsw", "xbar"], tag="swp")
                    S.op("vector",
                         (lambda c2=c2: lambda e: e.tensor_tensor(
                             a16(p_m, 128, 1024), a16(p_g, 128, 1024),
                             a16(c_cos + 1024 * c2, 128, 1024), ALU.mult))(),
                         reads=["diff", "c_cos", "mbuf"], writes=["mbuf"], tag="rc")
                    qt = p_QT if pj == "q" else p_KT
                    qk = "QT" if pj == "q" else "KT"
                    S.op("vector",
                         (lambda c2=c2: lambda e: e.tensor_tensor(
                             a16(p_diff, 128, 1024), a16(p_gsw, 128, 1024),
                             a16(c_sin + 1024 * c2, 128, 1024), ALU.mult))(),
                         reads=["gsw", "c_sin", "diff"], writes=["diff"], tag="rs")
                    S.op("vector",
                         (lambda qt=qt, c2=c2: lambda e: e.tensor_tensor(
                             a16(qt + 1024 * c2, 128, 1024), a16(p_m, 128, 1024),
                             a16(p_diff, 128, 1024), ALU.add))(),
                         reads=["diff", "mbuf"], writes=[f"{qk}{c2}"], tag="ra")

        if debug and h == 0:
            S.dma("qscalar", lambda e: e.dma_start(dbg["dbg_QT"], a16(p_QT, 128, 2048)),
                  reads=["QT0", "QT1"], tag="dbg")
            S.dma("qscalar", lambda e: e.dma_start(dbg["dbg_KT"], a16(p_KT, 128, 2048)),
                  reads=["KT0", "KT1"], tag="dbg")
            S.dma("qscalar", lambda e: e.dma_start(dbg["dbg_V"], a16(p_V, 128, 2048)),
                  reads=[f"Vb{b}" for b in range(16)], tag="dbg")

        for j in range(4):
            for b in range(4 * j + 4):
                k = b - 4 * j
                c0 = 512 * j + 128 * k if k >= 0 else 512 * j
                W = 512 * (j + 1) - c0
                pp = pST[b % 2]
                bkey = f"bank{4 + b % 2}"
                qks = (["QT0"] if c0 < 1024 else []) + \
                      (["QT1"] if c0 + W > 1024 else [])
                S.op("tensor",
                     (lambda pp=pp, b=b, c0=c0, W=W: lambda e: e.matmul(
                         psap(pp, 0, 128, W), a16(p_KT + 128 * b, 128, 128),
                         a16(p_QT + c0, 128, W), start=True, stop=False))(),
                     reads=["KT0", "KT1"] + qks, writes=[bkey], tag="st")
                if k >= 0:
                    S.op("tensor",
                         (lambda pp=pp: lambda e: e.matmul(
                             psap(pp, 0, 128, 128), a16(c_nid, 128, 128),
                             a16(c_strict, 128, 128), start=False, stop=False))(),
                         reads=["c_nid", "c_strict"], writes=[bkey], tag="msk")
                S.op("tensor",
                     (lambda pp=pp, h=h, c0=c0, W=W: lambda e: e.matmul(
                         psap(pp, 0, 128, W), a16(c_sel + 128 * h, 16, 128),
                         a16(c_negsc + c0, 16, W), start=False, stop=True))(),
                     reads=["c_sel", "negsc"], writes=[bkey], tag="mt")
                pt = p_PT[b % 3]
                S.op("scalar",
                     (lambda pp=pp, pt=pt, b=b, h=h, W=W: lambda e: e.activation(
                         a16(pt, 128, W), psap(pp, 0, 128, W), AF.Exp,
                         bias=a32(c_negcS + 16 * b + h, 128, 1), scale=RSQ))(),
                     reads=[bkey, f"negcS{b}"], writes=[f"PT{b % 3}"], tag="exp")
                lastb = (b == 4 * j + 3)
                S.op("tensor",
                     (lambda pt=pt, b=b, c0=c0, W=W, j=j, st=(b == 0), sp=lastb:
                      lambda e: e.matmul(
                          psap(pOTp, c0 - 512 * j, 128, W),
                          a16(p_V + 128 * b, 128, 128), a16(pt, 128, W),
                          start=st, stop=sp))(),
                     reads=[f"PT{b % 3}", f"Vb{b}"], writes=["bank6"], tag="pv")
                S.op("tensor",
                     (lambda pt=pt, b=b, c0=c0, W=W, j=j, st=(b == 0), sp=lastb:
                      lambda e: e.matmul(
                          psap(pMIS, c0 - 512 * j, 1, W),
                          a16(c_onc16, 128, 1), a16(pt, 128, W),
                          start=st, stop=sp))(),
                     reads=[f"PT{b % 3}", "c_onc16"], writes=["bank7"], tag="rsm")
            S.op("vector",
                 (lambda j=j: lambda e: e.tensor_copy(
                     a32(p_rsum + 512 * j, 1, 512), psap(pMIS, 0, 1, 512)))(),
                 reads=["bank7"], writes=["rsumr"], tag="rse")
            S.op("vector",
                 (lambda j=j: lambda e: e.reciprocal(
                     a32(p_rsum + 512 * j, 1, 512, p0=32),
                     a32(p_rsum + 512 * j, 1, 512)))(),
                 reads=["rsumr"], writes=["recr"], tag="rec")
            S.op("tensor",
                 (lambda j=j: lambda e: e.matmul(
                     psap(pMIS, 0, 128, 512), a32(c_onr32, 1, 128, p0=32),
                     a32(p_rsum + 512 * j, 1, 512, p0=32),
                     start=True, stop=True))(),
                 reads=["recr", "c_onr32", "rsumr"], writes=["bank7"], tag="rb")
            S.op("vector",
                 lambda e: e.tensor_copy(a32(p_recb, 128, 512),
                                         psap(pMIS, 0, 128, 512)),
                 reads=["bank7"], writes=["recb"], tag="rbe")
            S.op("vector",
                 (lambda j=j: lambda e: e.tensor_tensor(
                     a16(p_OTb + 512 * j, 128, 512), psap(pOTp, 0, 128, 512),
                     a32(p_recb, 128, 512), ALU.mult))(),
                 reads=["bank6", "recb"], writes=[f"OTb{j}"], tag="ote")
        dst = hbm(ot_d, (128 * h) * T, [[T, 128], [1, T]])
        S.dma("qscalar",
              (lambda d=dst: lambda e: e.dma_start(d, a16(p_OTb, 128, 2048)))(),
              reads=[f"OTb{j}" for j in range(4)], writes=[f"OTd{h}"], tag="stOT")
        if debug and h == 0:
            S.dma("qscalar", lambda e: e.dma_start(dbg["dbg_OT"],
                                                   a16(p_OTb, 128, 2048)),
                  reads=[f"OTb{j}" for j in range(4)], tag="dbg")

    for h in range(H):
        emit_head(h)
        emit_wprep1_chunk(8)

    S.barrier(tag="b_ph2")
    S.op("vector",
         lambda e: e.tensor_copy(a16(p_sqscr, 128, 512),
                                 a16(p_sqscr + 512, 128, 512)),
         reads=["sqscr"], writes=["sqscr"], tag="spacer")
    for w, base, cnt in (("w1", 0, C * FF), ("w2", 64, FF * C)):
        mc = 8 if w == "w1" else 9
        swc = 6 if w == "w1" else 7
        swic = 10 if w == "w1" else 11
        S.op("vector",
             (lambda base=base, mc=mc: lambda e: e.tensor_reduce(
                 a32(c_sml + mc - 8, 128, 1), a32(s_wp + base, 128, 64),
                 AX.X, ALU.add))(),
             reads=[f"wp{ti}" for ti in range(base, base + 64)],
             writes=[f"wpsum{w}"], tag="wps")
        S.op("tensor",
             (lambda mc=mc: lambda e: e.matmul(
                 psap(pMIS, 300, 1, 1), a32(c_sml + mc - 8, 128, 1),
                 a32(c_onc32, 128, 1), start=True, stop=True))(),
             reads=[f"wpsum{w}", "c_onc32"], writes=["bank7"], tag="swtot")
        S.op("vector",
             (lambda mc=mc, cnt=cnt: lambda e: e.tensor_scalar(
                 a32(c_sml + mc, 1, 1), psap(pMIS, 300, 1, 1), 1.0 / cnt, EPS,
                 ALU.mult, ALU.max))(),
             reads=["bank7"], writes=[f"mean{w}"], tag="swm")
        S.op("tensor",
             (lambda mc=mc: lambda e: e.matmul(
                 psap(pMIS, 302, 128, 1), a32(c_onr32, 1, 128),
                 a32(c_sml + mc, 1, 1), start=True, stop=True))(),
             reads=[f"mean{w}", "c_onr32"], writes=["bank7"], tag="swb")
        S.op("vector",
             (lambda swc=swc: lambda e: e.reciprocal(
                 a32(c_sml + swc, 128, 1), psap(pMIS, 302, 128, 1)))(),
             reads=["bank7"], writes=[f"swcol{w}"], tag="swc")
        S.op("scalar",
             (lambda swic=swic: lambda e: e.activation(
                 a32(c_sml + swic, 128, 1), psap(pMIS, 302, 128, 1), AF.Copy))(),
             reads=["bank7"], writes=[f"swicol{w}"], tag="swi")
        S.op("tensor",
             (lambda swc=swc: lambda e: e.transpose(
                 psap(pMIS, 304, 1, 128), a32(c_sml + swc, 128, 1),
                 a32(c_id32, 128, 128)))(),
             reads=[f"swcol{w}", "c_id32"], writes=["bank7"], tag="swrt")
        S.op("scalar",
             (lambda w=w: lambda e: e.activation(
                 a16(c_pT + (0 if w == "w1" else 128), 1, 128),
                 psap(pMIS, 304, 1, 128), AF.Copy))(),
             reads=["bank7"], writes=[f"swrow{w}"], tag="swre")

    SWCOL = {"w1": 6, "w2": 7}
    SWICOL = {"w1": 10, "w2": 11}

    wp2 = {"i": 0}

    def emit_wprep2_chunk(n):
        for _ in range(n):
            ti = wp2["i"]
            if ti >= len(wtiles):
                return
            wp2["i"] += 1
            src, w, qdst = wtiles[ti]
            buf = p_wstr_s2[ti % 2]
            bufk = "x32" if ti % 2 == 0 else "r1buf"
            S.dma("qscalar",
                  (lambda b=buf, s=src: lambda e: e.dma_start(a32(b, 128, 2048), s))(),
                  writes=[bufk], tag="wld2")
            S.op("vector",
                 (lambda b=buf, w=w: lambda e: e.tensor_scalar(
                     a32(p_rsum, 128, 2048), a32(b, 128, 2048),
                     a32(c_sml + SWCOL[w], 128, 1), RC, ALU.mult, ALU.add))(),
                 reads=[bufk, f"swcol{w}"], writes=["wtmp"], tag="wrnd")
            S.op("vector",
                 lambda e: e.tensor_scalar(a8ap(s_xq1i, 128, 2048),
                                           a32(p_rsum, 128, 2048), RC, None,
                                           ALU.subtract),
                 reads=["wtmp"], writes=["xqi"], tag="wrnd2")
            if debug and ti == 0:
                S.op("vector",
                     (lambda b=buf, w=w: lambda e: e.tensor_scalar(
                         a32(p_recb, 128, 512), a32(b, 128, 512),
                         a32(c_sml + SWCOL[w], 128, 1), None, ALU.mult))(),
                     reads=[bufk, f"swcol{w}", "xqi"], writes=["wi8d"], tag="dbg")
                S.dma("qscalar",
                      lambda e: e.dma_start(
                          bass.AP(dbg["dbg_wi8"].tensor, 0, [[2048, 128], [1, 512]]),
                          a32(p_recb, 128, 512)),
                      reads=["wi8d"], tag="dbg")
            S.op("vector",
                 lambda e: e.tensor_scalar(a16(p_wq16, 128, 2048),
                                           a8ap(s_xq1i, 128, 2048), 1.0, -1.0,
                                           ALU.min, ALU.max),
                 reads=["xqi"], writes=["wq16"], tag="wclip")
            S.dma("qgpsimd",
                  (lambda d=qdst: lambda e: e.dma_start(d, a16(p_wq16, 128, 2048)))(),
                  reads=["wq16"], writes=[], tag="wqst")

    p_wstr_s2 = [p_rsum, p_recb]
    p_wstr_s2 = [p_x32, p_r1]

    def emit_phase2():
        banksets = [["bank0", "bank1", "bank2", "bank3"],
                    ["bank4", "bank5", "bank6", "bank7"]]
        for tp in range(8):
            for h in range(16):
                ring = h % 2
                src = hbm(ot_d, (128 * h) * T + 256 * tp, [[T, 128], [1, 256]])
                dst = a16(p_OTsl[ring], 128, 256)
                S.dma("qsync",
                      (lambda d=dst, s=src: lambda e: e.dma_start(d, s))(),
                      reads=[f"OTd{h}"], writes=[f"OTsl{ring}"], tag="ldOT")
                rwo = h % 2
                srcw = hbm(WoT_d, (128 * h) * C, [[C, 128], [1, 2048]])
                dstw = a16(p_Wsl[2 + rwo], 128, 2048)
                S.dma("qgpsimd",
                      (lambda d=dstw, s=srcw: lambda e: e.dma_start(d, s))(),
                      writes=[f"Wsl{2 + rwo}"], tag="ldWo")
                for tt_i in range(2):
                    for cc in range(4):
                        dst_ap = (bigap(512 * cc) if tt_i == 0 else
                                  psap([pST[0], pST[1], pOTp, pMIS][cc], 0))
                        S.op("tensor",
                             (lambda h=h, tt_i=tt_i, cc=cc, ring=ring, rwo=rwo,
                                     dst_ap=dst_ap, st=(h == 0): lambda e: e.matmul(
                                 dst_ap, a16(p_OTsl[ring] + 128 * tt_i, 128, 128),
                                 a16(p_Wsl[2 + rwo] + 512 * cc, 128, 512),
                                 start=st, stop=(h == 15)))(),
                             reads=[f"Wsl{2 + rwo}", f"OTsl{ring}"],
                             writes=[banksets[tt_i][cc]], tag="wo")
            for tt_i in range(2):
                tt = 2 * tp + tt_i
                src_x = hbm(x_d, (128 * tt) * C, [[C, 128], [1, C]])
                S.dma("qscalar",
                      (lambda s=src_x: lambda e: e.dma_start(a32(p_x32, 128, 2048), s))(),
                      writes=["x32"], tag="ldx")
                for cc in range(4):
                    src_ap = (bigap(512 * cc) if tt_i == 0 else
                              psap([pST[0], pST[1], pOTp, pMIS][cc], 0))
                    S.op("vector",
                         (lambda src_ap=src_ap, cc=cc: lambda e: e.scalar_tensor_tensor(
                             a32(p_r1 + 512 * cc, 128, 512), src_ap, 1.0,
                             a32(p_x32 + 512 * cc, 128, 512), ALU.mult,
                             ALU.add))(),
                         reads=[banksets[tt_i][cc], "x32"],
                         writes=["r1buf"], tag="r1")
                S.op("scalar",
                     lambda e: e.activation(a16(p_sqscr, 128, 2048),
                                            a32(p_r1, 128, 2048), AF.Copy,
                                            accum_out=a32(p_st + 8, 128, 1)),
                     reads=["r1buf"], writes=["ssum", "sqscr"], tag="scp")
                S.op("scalar",
                     lambda e: e.activation(a16(p_sqscr, 128, 2048),
                                            a32(p_r1, 128, 2048), AF.Square,
                                            accum_out=a32(p_st + 9, 128, 1)),
                     reads=["r1buf"], writes=["ssq", "sqscr"], tag="sq")
                emit_ln_chain(p_st, "ssum", "ssq", C, EPS, "1")
                S.op("vector",
                     lambda e: e.tensor_scalar(a16(p_r1nst, 128, 2048),
                                               a32(p_r1, 128, 2048),
                                               a32(p_st + 16, 128, 1),
                                               a32(p_st + 17, 128, 1),
                                               ALU.mult, ALU.add),
                     reads=["r1buf", "rs1", "sh1"], writes=["r1ntile"],
                     tag="nrm")
                dstn = hbm(r1n_d, (128 * tt) * C, [[C, 128], [1, C]])
                S.dma("qsync",
                      (lambda d=dstn: lambda e: e.dma_start(d, a16(p_r1nst, 128, 2048)))(),
                      reads=["r1ntile"], writes=[f"r1nd{tt}"], tag="str1n")
                if debug and tt == 0:
                    S.dma("qscalar",
                          lambda e: e.dma_start(dbg["dbg_r1n"], a16(p_r1nst, 128, 2048)),
                          reads=["r1ntile"], tag="dbg")
                    S.dma("qscalar",
                          lambda e: e.dma_start(dbg["dbg_r1"], a32(p_r1, 128, 2048)),
                          reads=["r1buf", "r1ntile"], tag="dbg")
                    S.dma("qscalar",
                          lambda e: e.dma_start(dbg["dbg_st2"], a32(p_st, 128, 18)),
                          reads=["nmu1", "musq1", "var1", "lnv1", "hlv1",
                                 "rs1", "sh1", "ssum", "ssq", "r1ntile"],
                          tag="dbg")
            emit_wprep2_chunk(16)

    def emit_ln_chain(stc, sumk, sqk, NN, eps, sfx, spacer=None):
        if spacer is not None:
            S.op("vector",
                 (lambda sp=spacer: lambda e: e.tensor_copy(
                     sp[0](sp[1], 128, 512), sp[0](sp[1] + 512, 128, 512)))(),
                 reads=[spacer[2]], writes=list(spacer[2:]), tag="spacer")
        S.op("vector",
             (lambda stc=stc: lambda e: e.tensor_scalar(
                 a32(stc + 10, 128, 1), a32(stc + 8, 128, 1), -1.0 / NN, None,
                 ALU.mult))(),
             reads=[sumk], writes=["nmu" + sfx], tag="nmu")
        S.op("scalar",
             (lambda stc=stc: lambda e: e.activation(
                 a32(stc + 11, 128, 1), a32(stc + 10, 128, 1), AF.Square))(),
             reads=["nmu" + sfx], writes=["musq" + sfx], tag="musq")
        S.op("vector",
             (lambda stc=stc: lambda e: e.scalar_tensor_tensor(
                 a32(stc + 13, 128, 1), a32(stc + 9, 128, 1), 1.0 / NN,
                 a32(stc + 11, 128, 1), ALU.mult, ALU.subtract))(),
             reads=[sqk, "musq" + sfx], writes=["var" + sfx], tag="var")
        S.op("scalar",
             (lambda stc=stc: lambda e: e.activation(
                 a32(stc + 14, 128, 1), a32(stc + 13, 128, 1), AF.Ln,
                 bias=a32(c_onc32 + 15, 128, 1)))(),
             reads=["var" + sfx, "c_onc32"], writes=["lnv" + sfx], tag="lnv")
        S.op("vector",
             (lambda stc=stc: lambda e: e.tensor_scalar(
                 a32(stc + 15, 128, 1), a32(stc + 14, 128, 1), -0.5, None,
                 ALU.mult))(),
             reads=["lnv" + sfx], writes=["hlv" + sfx], tag="hlv")
        S.op("scalar",
             (lambda stc=stc: lambda e: e.activation(
                 a32(stc + 16, 128, 1), a32(stc + 15, 128, 1), AF.Exp))(),
             reads=["hlv" + sfx], writes=["rs" + sfx], tag="rsv")
        S.op("gpsimd",
             (lambda stc=stc: lambda e: e.tensor_tensor(
                 a32(stc + 17, 128, 1), a32(stc + 10, 128, 1),
                 a32(stc + 16, 128, 1), ALU.mult))(),
             reads=["nmu" + sfx, "rs" + sfx], writes=["sh" + sfx], tag="sh")

    emit_phase2()
    emit_wprep2_chunk(len(wtiles))

    S.barrier(tag="b_ph3")

    for src_d, p0, key in ((b1_d, 0, "rows_b1"), (g_d, 32, "rows_g"),
                           (be_d, 64, "rows_be")):
        for r in range(3):
            nch = len(range(r, 16, 3))
            src = hbm(src_d, 512 * r, [[1, 1], [512 * 3, nch], [1, 512]])
            dst = bass.AP(A16.t, p0 * A16.ne + s_rows + 512 * r,
                          [[A16.ne, 1], [512 * 3, nch], [1, 512]])
            S.dma("qgpsimd", (lambda d=dst, s=src: lambda e: e.dma_start(d, s))(),
                  writes=[key], tag="rowpk")
    for r in range(3):
        nch = len(range(r, 4, 3))
        src = hbm(b2_d, 512 * r, [[1, 1], [512 * 3, nch], [1, 512]])
        dst = bass.AP(A16.t, s_rows2 + 512 * r,
                      [[A16.ne, 1], [512 * 3, nch], [1, 512]])
        S.dma("qgpsimd", (lambda d=dst, s=src: lambda e: e.dma_start(d, s))(),
              writes=["rows_b2"], tag="rowpk")

    bkey = [f"bank{i}" for i in range(8)]

    def tb(ii, c):
        return s_tb + 16 * ii + c
    LN2C = p_st
    LN3C = p_st + 20

    for g in range(4):
        tiles = list(range(4 * g, 4 * g + 4))
        for ii, tt in enumerate(tiles):
            rb = s_r1n[ii % 2]
            src = hbm(r1n_d, (128 * tt) * C, [[C, 128], [1, C]])
            S.dma("qgpsimd",
                  (lambda d=rb, s=src: lambda e: e.dma_start(a16(d, 128, 2048), s))(),
                  reads=[f"r1nd{tt}"],
                  writes=[f"r1t{ii % 2}"]
                  + [f"r2_{2 + ii % 2}_{dc}" for dc in range(4)],
                  tag="ldr1n")
            S.op("vector",
                 (lambda rb=rb, ii=ii: lambda e: e.tensor_reduce(
                     a32(s_sc, 128, 1), a16(rb, 128, 2048), AX.X, ALU.max,
                     apply_absolute_value=True))(),
                 reads=[f"r1t{ii % 2}"], writes=["am1r"], tag="am1")
            S.op("scalar",
                 lambda e: e.activation(a32(s_sc + 2, 128, 1),
                                        a32(s_sc, 128, 1), AF.Ln,
                                        scale=1.0 / 127.0),
                 reads=["am1r"], writes=["lq1"], tag="lq1")
            S.op("vector",
                 lambda e: e.tensor_scalar(a32(s_sc + 3, 128, 1),
                                           a32(s_sc + 2, 128, 1), -1.0, None,
                                           ALU.mult),
                 reads=["lq1"], writes=["nq1"], tag="nq1")
            S.op("scalar",
                 (lambda ii=ii: lambda e: e.activation(
                     a32(tb(ii, 1), 128, 1), a32(s_sc + 3, 128, 1), AF.Exp))(),
                 reads=["nq1"], writes=[f"s1_{ii}"], tag="s1")
            S.op("vector",
                 lambda e: e.tensor_scalar(a32(s_sc + 4, 128, 1),
                                           a32(s_sc + 2, 128, 1), 1.0, None,
                                           ALU.mult),
                 reads=["lq1"], writes=["cq1"], tag="cq1")
            S.op("scalar",
                 (lambda ii=ii: lambda e: e.activation(
                     a32(tb(ii, 2), 128, 1), a32(s_sc + 4, 128, 1), AF.Exp))(),
                 reads=["cq1"], writes=[f"s1i_{ii}"], tag="s1i")
            S.op("vector",
                 (lambda rb=rb, ii=ii: lambda e: e.tensor_scalar(
                     a32(p_rsum, 128, 2048), a16(rb, 128, 2048),
                     a32(tb(ii, 1), 128, 1), RC, ALU.mult, ALU.add))(),
                 reads=[f"r1t{ii % 2}", f"s1_{ii}"], writes=["wtmp"], tag="rnd1")
            S.op("vector",
                 lambda e: e.tensor_scalar(a8ap(s_xq1i, 128, 2048),
                                           a32(p_rsum, 128, 2048), RC, None,
                                           ALU.subtract),
                 reads=["wtmp"], writes=["xqi"], tag="rnd1b")
            S.op("vector",
                 (lambda ii=ii: lambda e: e.tensor_scalar(
                     a16(s_xq1d, 128, 2048), a8ap(s_xq1i, 128, 2048),
                     a32(tb(ii, 2), 128, 1), None, ALU.mult))(),
                 reads=["xqi", f"s1i_{ii}"], writes=["xq1d", "Gq"], tag="deq1")
            for b in range(16):
                al1 = {0: ["Bq"], 1: ["vch"],
                       2: [f"r2_0_{dc}" for dc in range(4)],
                       3: [f"r2_1_{dc}" for dc in range(4)]}[ii]
                S.dma("qsync" if b % 2 == 0 else "qscalar",
                      (lambda b=b, ii=ii: lambda e: e.dma_start(
                          a16(s_xq1dT[ii] + 128 * b, 128, 128),
                          a16(s_xq1d + 128 * b, 128, 128), transpose=True))(),
                      reads=["xq1d"], writes=[f"x1T_{ii}"] + al1, tag="x1tr")

        for s16 in range(16):
            for half in range(2):
                src = hbm(w1q_d, (1024 * half) * FF + 512 * s16,
                          [[FF, 128], [128 * FF, 8], [1, 512]])
                dst = bass.AP(A16.t, s_w1sl[half],
                              [[A16.ne, 128], [512, 8], [1, 512]])
                S.dma("qgpsimd",
                      (lambda d=dst, s=src: lambda e: e.dma_start(d, s))(),
                      writes=[f"w1sl{half}", f"w2sl{half}"], tag="ldw1q")
            for ii in range(4):
                for cb in range(16):
                    half, cb8 = divmod(cb, 8)
                    S.op("tensor",
                         (lambda ii=ii, cb=cb, half=half, cb8=cb8,
                                 st=(cb == 0): lambda e: e.matmul(
                             bap(ii), a16(s_xq1dT[ii] + 128 * cb, 128, 128),
                             a16(s_w1sl[half] + 512 * cb8, 128, 512),
                             start=st, stop=False))(),
                         reads=[f"w1sl{half}", f"x1T_{ii}"], writes=[bkey[ii]],
                         tag="w1mm")
                S.op("tensor",
                     (lambda ii=ii, s16=s16: lambda e: e.matmul(
                         bap(ii), a16(c_pT, 1, 128),
                         a16(s_rows + 512 * s16, 1, 512), start=False,
                         stop=True))(),
                     reads=["swroww1", "rows_b1"], writes=[bkey[ii]],
                     tag="b1r1")
                S.op("scalar",
                     (lambda ii=ii, s16=s16: lambda e: e.activation(
                         a16(s_slot[ii] + 512 * s16, 128, 512), bap(ii),
                         AF.Gelu, scale=a32(c_sml + 10, 128, 1),
                         accum_out=a32(s_gp + 16 * ii + s16, 128, 1)))(),
                     reads=[bkey[ii], "swicolw1"],
                     writes=[f"h_{ii}_{s16 // 4}", f"gp_{ii}", f"SL{ii}"],
                     tag="gelu")

        if debug and g == 0:
            S.dma("qscalar",
                  lambda e: e.dma_start(dbg["dbg_hg"], a16(s_slot[0], 128, FF)),
                  reads=[f"h_0_{q}" for q in range(4)], tag="dbg")
        S.barrier(tag=f"b_g{g}mid")

        for ii in range(4):
            S.op("vector",
                 (lambda ii=ii: lambda e: e.tensor_reduce(
                     a32(LN2C + 8, 128, 1), a32(s_gp + 16 * ii, 128, 16),
                     AX.X, ALU.add))(),
                 reads=[f"gp_{ii}"], writes=["ssum2"], tag="gsum")
            for qq in range(4):
                S.op("scalar",
                     (lambda ii=ii, qq=qq: lambda e: e.activation(
                         a16(s_v, 128, 2048), a16(s_slot[ii] + 2048 * qq, 128, 2048),
                         AF.Square, accum_out=a32(s_sq + qq, 128, 1)))(),
                     reads=[f"h_{ii}_{qq}"],
                     writes=["vch", "x1T_1", f"sqp{qq}"], tag="sq2")
            S.op("vector",
                 lambda e: e.tensor_reduce(a32(LN2C + 9, 128, 1),
                                           a32(s_sq, 128, 4), AX.X, ALU.add),
                 reads=["sqp0", "sqp1", "sqp2", "sqp3"], writes=["ssq2"],
                 tag="sqr")
            emit_ln_chain(LN2C, "ssum2", "ssq2", FF, EPS, "2",
                          spacer=(a16, s_v, "vch", "x1T_1"))
            S.op("scalar",
                 (lambda ii=ii: lambda e: e.activation(
                     a32(tb(ii, 10), 128, 1), a32(LN2C + 10, 128, 1), AF.Copy,
                     scale=-1.0))(),
                 reads=["nmu2"], writes=[f"mu2_{ii}"], tag="mu2p")
            S.op("vector",
                 (lambda ii=ii: lambda e: e.tensor_copy(
                     a32(tb(ii, 11), 128, 1), a32(LN2C + 16, 128, 1)))(),
                 reads=["rs2"], writes=[f"rs2_{ii}"], tag="rs2p")

        for q in range(4):
            for dstc, rp0, rkey, okey in ((s_G, 32, "rows_g", "Gq"),
                                          (s_B, 64, "rows_be", "Bq")):
                for cc in range(4):
                    S.op("tensor",
                         (lambda rp0=rp0, q=q, cc=cc: lambda e: e.matmul(
                             psap(pST[0], 0), a16(c_onr16, 1, 128, p0=rp0),
                             a16(s_rows + 2048 * q + 512 * cc, 1, 512, p0=rp0),
                             start=True, stop=True))(),
                         reads=[rkey, "c_onr16"], writes=["bank4"], tag="gbm")
                    S.op("vector",
                         (lambda dstc=dstc, cc=cc: lambda e: e.tensor_copy(
                             a16(dstc + 512 * cc, 128, 512), psap(pST[0], 0)))(),
                         reads=["bank4"],
                         writes=[okey, "xq1d" if okey == "Gq" else "x1T_0"],
                         tag="gbe")
            for ii in range(4):
                S.op("vector",
                     (lambda ii=ii, q=q: lambda e: e.scalar_tensor_tensor(
                         a16(s_v, 128, 2048), a16(s_slot[ii] + 2048 * q, 128, 2048),
                         a32(tb(ii, 10), 128, 1), a16(s_G, 128, 2048),
                         ALU.subtract, ALU.mult))(),
                     reads=[f"h_{ii}_{q}", f"mu2_{ii}", "Gq", "vch"],
                     writes=["vch", "x1T_1"], tag="g1")
                S.op("vector",
                     (lambda ii=ii, q=q: lambda e: e.scalar_tensor_tensor(
                         a16(s_slot[ii] + 2048 * q, 128, 2048), a16(s_v, 128, 2048),
                         a32(tb(ii, 11), 128, 1), a16(s_B, 128, 2048),
                         ALU.mult, ALU.add))(),
                     reads=["vch", f"rs2_{ii}", "Bq"],
                     writes=[f"h2_{ii}_{q}", f"SL{ii}"], tag="g2")
                S.op("vector",
                     (lambda ii=ii, q=q: lambda e: e.tensor_reduce(
                         a32(s_gp + 16 * ii + q, 128, 1),
                         a16(s_slot[ii] + 2048 * q, 128, 2048),
                         AX.X, ALU.max, apply_absolute_value=True))(),
                     reads=[f"h2_{ii}_{q}"], writes=[f"gp_{ii}"], tag="am2")

        for ii in range(4):
            S.op("scalar",
                 (lambda ii=ii: lambda e: e.activation(
                     a32(s_gp + 16 * ii + 4, 128, 1),
                     a32(c_onc32 + 15, 128, 1), AF.Copy))(),
                 reads=["c_onc32"], writes=[f"gp_{ii}"], tag="epsc")
            S.op("vector",
                 (lambda ii=ii: lambda e: e.tensor_reduce(
                     a32(s_sc + 8, 128, 1), a32(s_gp + 16 * ii, 128, 5),
                     AX.X, ALU.max))(),
                 reads=[f"gp_{ii}"], writes=[f"am2_{ii}"], tag="am2r")
            S.op("scalar",
                 lambda e: e.activation(a32(s_sc + 10, 128, 1),
                                        a32(s_sc + 8, 128, 1), AF.Ln,
                                        scale=1.0 / 127.0),
                 reads=[f"am2_{ii}"], writes=["lq2"], tag="lq2")
            S.op("vector",
                 lambda e: e.tensor_scalar(a32(s_sc + 11, 128, 1),
                                           a32(s_sc + 10, 128, 1), -1.0, None,
                                           ALU.mult),
                 reads=["lq2"], writes=["nq2"], tag="nq2")
            S.op("scalar",
                 (lambda ii=ii: lambda e: e.activation(
                     a32(tb(ii, 5), 128, 1), a32(s_sc + 11, 128, 1), AF.Exp))(),
                 reads=["nq2"], writes=[f"s2_{ii}"], tag="s2")
            S.op("vector",
                 lambda e: e.tensor_scalar(a32(s_sc + 12, 128, 1),
                                           a32(s_sc + 10, 128, 1), 1.0, None,
                                           ALU.mult),
                 reads=["lq2"], writes=["cq2"], tag="cq2")
            S.op("scalar",
                 (lambda ii=ii: lambda e: e.activation(
                     a32(tb(ii, 6), 128, 1), a32(s_sc + 12, 128, 1), AF.Exp))(),
                 reads=["cq2"], writes=[f"s2i_{ii}"], tag="s2i")
            for qq in range(4):
                S.op("vector",
                     (lambda ii=ii, qq=qq: lambda e: e.tensor_scalar(
                         a32(p_rsum, 128, 2048),
                         a16(s_slot[ii] + 2048 * qq, 128, 2048),
                         a32(tb(ii, 5), 128, 1), RC, ALU.mult, ALU.add))(),
                     reads=[f"h2_{ii}_{qq}", f"s2_{ii}"], writes=["wtmp"],
                     tag="rnd2")
                S.op("vector",
                     (lambda qq=qq: lambda e: e.tensor_scalar(
                         a8ap(s_xq2i + 2048 * qq, 128, 2048),
                         a32(p_rsum, 128, 2048), RC, None, ALU.subtract))(),
                     reads=["wtmp"], writes=["xqi"], tag="rnd2b")
            S.op("vector",
                 (lambda ii=ii: lambda e: e.tensor_scalar(
                     a16(s_slot[ii], 128, 8192), a8ap(s_xq2i, 128, 8192),
                     a32(tb(ii, 6), 128, 1), None, ALU.mult))(),
                 reads=["xqi", f"s2i_{ii}"],
                 writes=[f"xq2d_{ii}", f"SL{ii}"], tag="deq2")
            dsl = (ii + 4) % 5
            for b in range(64):
                S.dma("qsync" if b % 2 == 0 else "qscalar",
                      (lambda ii=ii, b=b, dsl=dsl: lambda e: e.dma_start(
                          a16(s_slot[dsl] + 128 * b, 128, 128),
                          a16(s_slot[ii] + 128 * b, 128, 128), transpose=True))(),
                      reads=[f"xq2d_{ii}", f"SL{ii}"],
                      writes=[f"x2T_{ii}", f"SL{dsl}"], tag="x2tr")
            if debug and g == 0 and ii == 0:
                S.dma("qscalar",
                      lambda e: e.dma_start(dbg["dbg_xq2"],
                                            a16(s_slot[4], 128, 8192)),
                      reads=["x2T_0"], tag="dbg")
                S.dma("qscalar",
                      lambda e: e.dma_start(dbg["dbg_h0"],
                                            a16(s_slot[0], 128, 8192)),
                      reads=["xq2d_0"], tag="dbg")

        if debug and g == 0:
            S.dma("qscalar",
                  lambda e: e.dma_start(dbg["dbg_swc"], a32(c_sml + 6, 128, 2)),
                  reads=["swcolw1", "swcolw2"], tag="dbg")
            S.dma("qscalar",
                  lambda e: e.dma_start(dbg["dbg_sml"], a32(c_sml + 6, 1, 6)),
                  reads=["swicolw1", "swicolw2", "swcolw1", "swcolw2",
                         "meanw1", "meanw2"], tag="dbg")
            S.dma("qscalar",
                  lambda e: e.dma_start(dbg["dbg_tb2"], a32(s_tb + 4, 128, 3)),
                  reads=["s2_0", "s2i_0", "am2_0"], tag="dbg")
            S.dma("qscalar",
                  lambda e: e.dma_start(dbg["dbg_tb3"], a32(s_tb + 10, 128, 2)),
                  reads=["mu2_0", "rs2_0"], tag="dbg")
            S.dma("qsync",
                  lambda e: e.dma_start(dbg["dbg_w1q"],
                                        hbm(w1q_d, 0, [[FF, 128], [1, FF]])),
                  reads=["b_dummy"] if False else [], tag="dbg")

        for dc in range(4):
            for fs in range(8):
                src = hbm(w2q_d, (1024 * fs) * C + 512 * dc,
                          [[C, 128], [128 * C, 8], [1, 512]])
                dst = bass.AP(A16.t, s_w2sl[fs % 2],
                              [[A16.ne, 128], [512, 8], [1, 512]])
                S.dma("qgpsimd",
                      (lambda d=dst, s=src: lambda e: e.dma_start(d, s))(),
                      writes=[f"w2sl{fs % 2}", f"w1sl{fs % 2}"], tag="ldw2q")
                for ii in range(4):
                    dsl = (ii + 4) % 5
                    for f8 in range(8):
                        fb = 8 * fs + f8
                        S.op("tensor",
                             (lambda ii=ii, fb=fb, f8=f8, fs=fs, dsl=dsl,
                                     st=(fb == 0): lambda e: e.matmul(
                                 bap(ii), a16(s_slot[dsl] + 128 * fb, 128, 128),
                                 a16(s_w2sl[fs % 2] + 512 * f8, 128, 512),
                                 start=st, stop=False))(),
                             reads=[f"w2sl{fs % 2}", f"x2T_{ii}", f"SL{dsl}"],
                             writes=[bkey[ii]], tag="w2mm")
            for ii, tt in enumerate(tiles):
                S.op("tensor",
                     (lambda ii=ii, dc=dc: lambda e: e.matmul(
                         bap(ii), a16(c_pT + 128, 1, 128),
                         a16(s_rows2 + 512 * dc, 1, 512), start=False,
                         stop=True))(),
                     reads=["swroww2", "rows_b2"], writes=[bkey[ii]],
                     tag="b2r1")
                rcb = s_r1nc[ii % 2]
                src = hbm(r1n_d, (128 * tt) * C + 512 * dc, [[C, 128], [1, 512]])
                S.dma("qgpsimd",
                      (lambda d=rcb, s=src: lambda e: e.dma_start(
                          a16(d, 128, 512), s))(),
                      reads=[f"r1nd{tt}"], writes=[f"r1c{ii % 2}"], tag="ldr1c")
                S.op("vector",
                     (lambda ii=ii, dc=dc, rcb=rcb: lambda e: e.scalar_tensor_tensor(
                         a16(s_r2[ii] + 512 * dc, 128, 512), bap(ii),
                         a32(c_sml + 11, 128, 1), a16(rcb, 128, 512),
                         ALU.mult, ALU.add))(),
                     reads=[bkey[ii], f"r1c{ii % 2}", "swicolw2"],
                     writes=[f"r2_{ii}_{dc}",
                             ["x1T_2", "x1T_3", "r1t0", "r1t1"][ii]],
                     tag="y2ev")

        for ii, tt in enumerate(tiles):
            S.op("scalar",
                 (lambda ii=ii: lambda e: e.activation(
                     a16(s_v, 128, 2048), a16(s_r2[ii], 128, 2048), AF.Copy,
                     accum_out=a32(LN3C + 8, 128, 1)))(),
                 reads=[f"r2_{ii}_{dc}" for dc in range(4)],
                 writes=["ssum3", "vch", "x1T_1"], tag="r3s")
            S.op("scalar",
                 (lambda ii=ii: lambda e: e.activation(
                     a16(s_v, 128, 2048), a16(s_r2[ii], 128, 2048), AF.Square,
                     accum_out=a32(LN3C + 9, 128, 1)))(),
                 reads=[f"r2_{ii}_{dc}" for dc in range(4)],
                 writes=["ssq3", "vch", "x1T_1"], tag="r3q")
            emit_ln_chain(LN3C, "ssum3", "ssq3", C, EPS, "3")
            for cc in range(4):
                ob = s_out[cc % 2]
                S.op("vector",
                     (lambda ii=ii, cc=cc, ob=ob: lambda e: e.tensor_scalar(
                         a32(ob, 128, 512), a16(s_r2[ii] + 512 * cc, 128, 512),
                         a32(LN3C + 16, 128, 1), a32(LN3C + 17, 128, 1),
                         ALU.mult, ALU.add))(),
                     reads=[f"r2_{ii}_{cc}", "rs3", "sh3"],
                     writes=[f"outb{cc % 2}"], tag="on")
                dst = hbm(out_d, (128 * tt) * C + 512 * cc, [[C, 128], [1, 512]])
                S.dma("qscalar",
                      (lambda d=dst, ob=ob: lambda e: e.dma_start(d, a32(ob, 128, 512)))(),
                      reads=[f"outb{cc % 2}"], writes=[], tag="stout")

    import contextlib
    stack = contextlib.ExitStack()
    sems = {}
    for e in ENGINES:
        sems["E" + e] = stack.enter_context(nc.semaphore("sE" + e))
    for q in ("qsync", "qscalar", "qgpsimd"):
        for r in range(NQSEM):
            sems[f"Q{q}_{r}"] = stack.enter_context(
                nc.semaphore(f"s{q}{r}"))
    try:
        S.emit(nc, sems)
    finally:
        stack.close()
    return nc


def _consts():
    half = D // 2
    inv = 1.0 / (10000.0 ** (np.arange(0, D, 2, dtype=np.float32) / D))
    freqs = np.outer(np.arange(T, dtype=np.float32), inv)
    cos = np.cos(freqs).T.astype(np.float16)
    sin = np.sin(freqs).T.astype(np.float16)
    COSF = np.concatenate([cos, cos], axis=0)
    SINF = np.concatenate([sin, -sin], axis=0)
    SEL = np.zeros((80, 16 * 128), np.float16)
    for h in range(16):
        for p0 in (0, 32, 64):
            SEL[p0 + h, 128 * h:128 * h + 128] = 1.0
    kk = np.arange(128)[:, None]
    tt = np.arange(128)[None, :]
    STRICT = (60000.0 * (kk > tt)).astype(np.float16)
    NEGID = (-np.eye(128)).astype(np.float16)
    ID32 = np.eye(128, dtype=np.float32)
    ONR16 = np.zeros((65, 128), np.float16)
    ONR16[0] = 1.0; ONR16[32] = 1.0; ONR16[64] = 1.0
    ONR32 = ONR16.astype(np.float32)
    ONC16 = np.ones((128, 16), np.float16)
    ONC32 = np.ones((128, 16), np.float32)
    ONC32[:, 15] = 1e-5
    ZC16 = np.zeros((128, 16), np.float16)
    return dict(COSF=COSF, SINF=SINF, SEL16=SEL, STRICT=STRICT, NEGID=NEGID,
                ID32=ID32, ONR16=ONR16, ONR32=ONR32, ONC16=ONC16, ONC32=ONC32,
                ZC16=ZC16)


def kernel(L0_attn_x, L0_attn_Wq, L0_attn_Wsq, L0_attn_Wk, L0_attn_Wsk,
           L0_attn_Wv, L0_attn_Wsv, L0_attn_Wf, L0_attn_bf, L0_attn_Wo,
           L0_ffn_w1, L0_ffn_b1, L0_ffn_gamma, L0_ffn_beta, L0_ffn_w2,
           L0_ffn_b2, _debug=0):
    cst = _consts()
    f32 = np.float32
    shared = dict(
        WqT=np.ascontiguousarray(np.asarray(L0_attn_Wq, f32).T),
        WkT=np.ascontiguousarray(np.asarray(L0_attn_Wk, f32).T),
        WvT=np.ascontiguousarray(np.asarray(L0_attn_Wv, f32).T),
        WoT=np.ascontiguousarray(np.asarray(L0_attn_Wo, f32).T),
        WsT=np.ascontiguousarray(np.concatenate(
            [np.asarray(L0_attn_Wsq, f32).T, np.asarray(L0_attn_Wsk, f32).T,
             np.asarray(L0_attn_Wsv, f32).T, np.asarray(L0_attn_Wf, f32).T],
            axis=1)),
        bfrow=np.concatenate([np.zeros(48, f32),
                              np.asarray(L0_attn_bf, f32)])[None, :],
        w1T=np.ascontiguousarray(np.asarray(L0_ffn_w1, f32).T),
        w2T=np.ascontiguousarray(np.asarray(L0_ffn_w2, f32).T),
        b1row=np.ascontiguousarray(np.asarray(L0_ffn_b1, f32)[None, :]),
        b2row=np.ascontiguousarray(np.asarray(L0_ffn_b2, f32)[None, :]),
        grow=np.ascontiguousarray(np.asarray(L0_ffn_gamma, f32)[None, :]),
        brow=np.ascontiguousarray(np.asarray(L0_ffn_beta, f32)[None, :]),
        **cst)
    xs = np.asarray(L0_attn_x, f32)
    in_maps = []
    for b in range(8):
        m = dict(shared)
        m["x"] = np.ascontiguousarray(xs[b])
        m["xT"] = np.ascontiguousarray(xs[b].T)
        in_maps.append(m)
    nc = build(debug=_debug)
    res = run_bass_kernel_spmd(nc, in_maps, core_ids=list(range(8)))
    out = np.stack([np.asarray(r["out"]) for r in res.results], axis=0)
    if _debug:
        kernel._dbg = res.results
    return out
